# version 01
# baseline (speedup 1.0000x reference)
import contextlib
import numpy as np
import concourse.bass as bass
import concourse.mybir as mybir
from concourse.bass_utils import run_bass_kernel_spmd

F32 = mybir.dt.float32
BF16 = mybir.dt.bfloat16
AF = mybir.ActivationFunctionType
ALU = mybir.AluOpType
AX = mybir.AxisListType

B, SEQ, D = 2, 4096, 2048
NCORES = 8
P = 128
TPC = 8
TOK = TPC * P
NT = SEQ // P
NF = D // P
HEADS, KVHEADS, HD = 32, 8, 64
KVD = KVHEADS * HD
ROPE_THETA = 10000.0
ENGINES = ("sync", "tensor", "vector", "scalar", "gpsimd")



class SC:

    def __init__(self, handle, name):
        self.h = handle
        self.name = name
        self.n = 0

    @property
    def tok(self):
        return (self, self.n)


class Prog:

    ENGINE_SEM = {"vector": "dve", "scalar": "act", "gpsimd": "poo"}

    def __init__(self, nc, est):
        self.nc = nc
        self.est = est
        self.q = {e: [] for e in ENGINES}
        self.sems = {}
        self._fence_pend = {}

    def sem(self, name):
        if name not in self.sems:
            self.sems[name] = SC(self.est.enter_context(self.nc.semaphore(name)), name)
        return self.sems[name]

    def op(self, engine, fn, waits=(), incs=(), is_dma=False):
        w = [t for t in waits if t is not None]
        if engine in self._fence_pend:
            w = self._fence_pend.pop(engine) + w
        incs = list(incs)
        gname = self.ENGINE_SEM.get(engine)
        if gname is not None and not is_dma:
            g = self.sem(gname)
            if not any(sc is g for sc, _ in incs):
                incs.append((g, 1))
        for sc, k in incs:
            sc.n += k
        self.q[engine].append((w, fn, incs, is_dma))

    def dma(self, engine, out, in_, sem, waits=()):
        self.op(engine, lambda e, o=out, i=in_: e.dma_start(out=o, in_=i),
                waits=waits, incs=[(sem, 16)], is_dma=True)

    def chain(self, engine, steps, waits=(), incs=()):
        for i, fn in enumerate(steps):
            self.op(engine, fn, waits=waits if i == 0 else (),
                    incs=incs if i == len(steps) - 1 else ())
        g = self.sems[self.ENGINE_SEM[engine]]
        return g.tok

    def fence(self):
        toks = [(sc, sc.n) for sc in self.sems.values() if sc.n > 0]
        for e in ENGINES:
            self._fence_pend[e] = list(toks)

    def finalize(self):
        final_toks = [(sc, sc.n) for sc in self.sems.values() if sc.n > 0]
        self.q["gpsimd"].append((final_toks, lambda e: e.engine_nop(), [], False))
        with self.nc.Block() as blk:
            for e in ENGINES:
                ops = self.q[e]
                if not ops:
                    continue
                gname = self.ENGINE_SEM.get(e)
                gate = self.sems.get(gname) if gname else None

                def body(eng, ops=ops, gate=gate):
                    last_seen = {}
                    nauto = 0
                    for w, fn, incs, is_dma in ops:
                        if gate is not None and not is_dma and nauto > 0:
                            eng.wait_ge(gate.h, nauto)
                        for sc, v in w:
                            if v > 0 and last_seen.get(sc.name, -1) < v:
                                eng.wait_ge(sc.h, v)
                                last_seen[sc.name] = v
                        ins = fn(eng)
                        for sc, k in incs:
                            ins.then_inc(sc.h, k)
                        if gate is not None and not is_dma:
                            nauto += 1

                getattr(blk, e)(body)


class Arena:

    def __init__(self, tensor, cols, name):
        self.t = tensor
        self.cols = cols
        self.top = 0
        self.name = name
        self.marks = []

    def alloc(self, ncols, tag=""):
        ncols = (ncols + 15) // 16 * 16
        if self.top + ncols > self.cols:
            raise RuntimeError(
                f"{self.name} arena overflow: {self.top}+{ncols}>{self.cols} ({tag})")
        ap = self.t[:, self.top:self.top + ncols]
        self.top += ncols
        return ap

    def push(self):
        self.marks.append(self.top)

    def pop(self):
        self.top = self.marks.pop()



def _deint64():
    pos = np.zeros(64, dtype=np.int64)
    for d in range(64):
        pos[d] = d // 2 if d % 2 == 0 else 32 + d // 2
    return pos


def _slot_maps():
    head_of_slot = np.zeros(32, dtype=np.int64)
    for j in range(KVHEADS):
        for i in range(4):
            h = 4 * j + i
            chunk = (j // 2) * 4 + i
            s = chunk * 2 + (j % 2)
            head_of_slot[s] = h
    return head_of_slot


def host_l0_perms(wq, wk, wo):
    dp = _deint64()
    hos = _slot_maps()
    perm_q = np.zeros(D, dtype=np.int64)
    perm_o = np.zeros(D, dtype=np.int64)
    for s in range(32):
        h = int(hos[s])
        base = 128 * (s // 2) + 64 * (s % 2)
        for d in range(64):
            perm_q[base + dp[d]] = 64 * h + d
            perm_o[base + d] = 64 * h + d
    perm_k = np.zeros(KVD, dtype=np.int64)
    for j in range(KVHEADS):
        for d in range(64):
            perm_k[64 * j + dp[d]] = 64 * j + d
    return (np.ascontiguousarray(wq[:, perm_q]),
            np.ascontiguousarray(wk[:, perm_k]),
            np.ascontiguousarray(wo[perm_o, :]))


def host_rope_tables(cols):
    freq = ROPE_THETA ** (-(np.arange(32, dtype=np.float64) / 32.0))
    ph = np.asarray(cols, dtype=np.float64)[None, :] * freq[:, None]
    c, s = np.cos(ph), np.sin(ph)
    T1 = np.tile(c, (4, 1)).astype(np.float32)
    T2 = np.concatenate([-s, s, -s, s], axis=0).astype(np.float32)
    return np.ascontiguousarray(T1), np.ascontiguousarray(T2)


def host_swap_pt():
    sig = np.concatenate([np.arange(32, 64), np.arange(0, 32),
                          np.arange(96, 128), np.arange(64, 96)])
    Pt = np.zeros((P, P), dtype=np.float32)
    for pp in range(P):
        Pt[sig[pp], pp] = 1.0
    return Pt


def host_dmask(g):
    out = np.zeros((TPC * P, 4 * 8 * P), dtype=np.float32)
    for j in range(TPC):
        tq = 4 * j + g
        for dt in range(4):
            T = 4 * j + dt
            k = np.arange(P)[:, None] + T * P
            q = np.arange(P)[None, :] + tq * P
            blk = (k <= q).astype(np.float32)
            for s in range(8):
                out[j * P:(j + 1) * P, dt * 1024 + s * P: dt * 1024 + (s + 1) * P] = blk
    return out


def _kv_of_slot():
    hos = _slot_maps()
    return [int(hos[s]) // 4 for s in range(32)]



def emit_ln(p, y_ap, out_ap, sq_scr, stats, eps, width, wait_y=(),
            affine=None, extra_mul=None, extra_out=None):
    dve, act = p.sem("dve"), p.sem("act")
    p.op("scalar", lambda e: e.activation(sq_scr, y_ap, AF.Square),
         waits=wait_y, incs=[(act, 1)])
    tok_sq = act.tok

    def red0(e):
        e.tensor_reduce(stats[:, 0:1], y_ap, axis=AX.X, op=ALU.add)
        e.tensor_reduce(stats[:, 1:2], sq_scr, axis=AX.X, op=ALU.add)
        return e.tensor_scalar_mul(stats[:, 2:3], stats[:, 0:1], 1.0 / width)
    p.chain("vector", [
        red0,
        lambda e: e.tensor_mul(stats[:, 3:4], stats[:, 2:3], stats[:, 2:3]),
        lambda e: e.scalar_tensor_tensor(stats[:, 4:5], stats[:, 1:2], 1.0 / width,
                                         stats[:, 3:4], ALU.mult, ALU.subtract),
        lambda e: e.tensor_scalar_add(stats[:, 5:6], stats[:, 4:5], eps),
        lambda e: e.reciprocal(stats[:, 6:7], stats[:, 5:6]),
    ], waits=list(wait_y) + [tok_sq], incs=[(dve, 1)])
    p.op("scalar", lambda e: e.activation(stats[:, 7:8], stats[:, 6:7], AF.Sqrt),
         waits=[dve.tok], incs=[(act, 1)])
    tok_rstd = act.tok

    def apply(e):
        ins = e.tensor_scalar(out_ap, y_ap, stats[:, 2:3], stats[:, 7:8],
                              ALU.subtract, ALU.mult)
        if affine is not None:
            g_ap, b_ap = affine
            e.tensor_mul(out_ap, out_ap, g_ap)
            ins = e.tensor_add(out_ap, out_ap, b_ap)
        if extra_mul is not None:
            ins = e.tensor_mul(out_ap, out_ap, extra_mul)
        if extra_out is not None:
            ins = e.tensor_copy(extra_out, out_ap)
        return ins
    p.op("vector", apply, waits=[tok_rstd], incs=[(dve, 1)])
    return dve.tok



OD = 4 * D
NOB = OD // P


def _emit_l0_ffn(p, ar, arh, io, ps):
    ldc, ldx, ldw, ldt, ldh = (p.sem(x) for x in ("ldc", "ldx", "ldw", "ldt", "ldh"))
    pe, dve, act = p.sem("pe"), p.sem("dve"), p.sem("act")
    stw, sth, sto = p.sem("stw"), p.sem("sth"), p.sem("sto")

    XS = ar.alloc(TPC * D, "XS")
    idf = ar.alloc(P, "ident")
    ones = ar.alloc(P, "ones")
    stats = ar.alloc(16, "stats")
    gbuf = ar.alloc(16, "gbuf")
    S_sb = ar.alloc(NF, "S_sb")
    srow = ar.alloc(TOK, "srowacc")
    srow2 = ar.alloc(TOK, "srow2acc")
    idb = arh.alloc(P, "identb")

    p.dma("sync", idf, io["ident"], ldc)
    p.dma("sync", ones, io["ones"], ldc)
    p.dma("sync", idb, io["identb"], ldc)
    for j in range(TPC):
        p.dma("sync", XS[:, j * D:(j + 1) * D], io["xin"][j * P:(j + 1) * P, :], ldx)
    tok_c = ldc.tok
    tok_x = ldx.tok

    ar.push()
    wb = [ar.alloc(D, "wb0"), ar.alloc(D, "wb1")]
    gacc = ar.alloc(4, "gacc")
    tmp1 = ar.alloc(4, "gtmp")
    p.op("vector", lambda e: e.memset(gacc, 0.0), incs=[(dve, 1)])
    blocks = [(0, io["wfc"][t * P:(t + 1) * P, :]) for t in range(NOB)]
    blocks += [(1, io["wpr"][t * P:(t + 1) * P, c * D:(c + 1) * D])
               for t in range(NF) for c in range(4)]
    ldwp = [p.sem("ldw0"), p.sem("ldw1")]
    red_toks = []
    for i, (col, src) in enumerate(blocks):
        p.dma("sync", wb[i % 2], src, ldwp[i % 2],
              waits=[red_toks[i - 2]] if i >= 2 else [])
        tok_ld = ldwp[i % 2].tok

        p.chain("vector", [
            (lambda e, b=i % 2: e.tensor_reduce(tmp1[:, 0:1], wb[b], axis=AX.X,
                                                op=ALU.add, apply_absolute_value=True)),
            (lambda e, c=col: e.tensor_add(gacc[:, c:c + 1], gacc[:, c:c + 1],
                                           tmp1[:, 0:1])),
        ], waits=[tok_ld], incs=[(dve, 1)])
        red_toks.append(dve.tok)
    for g_col, t_col, nt_col in [(0, 1, 2), (3, 4, 5)]:
        p.op("tensor", lambda e, c=g_col: e.matmul(
            ps["Dk"][0:1, 0:1], gacc[:, (0 if c == 0 else 1):(1 if c == 0 else 2)],
            ones[:, 0:1], start=True, stop=True),
            waits=[dve.tok, tok_c], incs=[(pe, 1)])
        p.op("vector", lambda e: e.tensor_copy(tmp1[0:1, 2:3], ps["Dk"][0:1, 0:1]),
             waits=[pe.tok], incs=[(dve, 1)])
        p.op("tensor", lambda e: e.matmul(ps["Dk"][:, 1:2], ones[0:1, 0:P],
                                          tmp1[0:1, 2:3], start=True, stop=True),
             waits=[dve.tok], incs=[(pe, 1)])

        p.chain("vector", [
            (lambda e, c=g_col: e.tensor_scalar_mul(gbuf[:, c:c + 1],
                                                    ps["Dk"][:, 1:2], 1.0 / (OD * D))),
            (lambda e, g=g_col, t=t_col: e.tensor_scalar_add(
                gbuf[:, t:t + 1], gbuf[:, g:g + 1], 1e-6)),
            (lambda e, t=t_col: e.tensor_scalar_mul(gbuf[:, t:t + 1],
                                                    gbuf[:, t:t + 1], 0.5)),
            (lambda e, t=t_col, nt=nt_col: e.tensor_scalar_mul(
                gbuf[:, nt:nt + 1], gbuf[:, t:t + 1], -1.0)),
        ], waits=[pe.tok], incs=[(dve, 1)])
    tok_gam = dve.tok
    ar.pop()

    ar.push()
    arh.push()
    wb = [ar.alloc(D, "twb0"), ar.alloc(D, "twb1")]
    ttmp = ar.alloc(2 * D, "terntmp")
    tb = arh.alloc(4 * D, "tb")
    tt = arh.alloc(D, "tt")
    tern_toks = []
    prev_ev = None
    prev_st = None
    i = 0
    for wi in range(2):
        nblk = NOB if wi == 0 else NF
        nsub = 1 if wi == 0 else 4
        nch = NF if wi == 0 else 4 * NF
        for t in range(nblk):
            for c in range(nsub):
                src = (io["wfc"][t * P:(t + 1) * P, :] if wi == 0
                       else io["wpr"][t * P:(t + 1) * P, c * D:(c + 1) * D])
                w = [tern_toks[i - 2]] if i >= 2 else [tok_gam]
                p.dma("sync", wb[i % 2], src, ldwp[i % 2], waits=w)
                tok_ld = ldwp[i % 2].tok

                def tern(e, b=i % 2, co=c, wi=wi):
                    tcol = gbuf[:, 1:2] if wi == 0 else gbuf[:, 4:5]
                    ntcol = gbuf[:, 2:3] if wi == 0 else gbuf[:, 5:6]
                    dst = tb[:, co * D:(co + 1) * D]
                    e.tensor_single_scalar(ttmp[:, 0:D], wb[b], tcol, ALU.is_gt)
                    e.tensor_single_scalar(ttmp[:, D:2 * D], wb[b], ntcol, ALU.is_lt)
                    return e.tensor_sub(dst, ttmp[:, 0:D], ttmp[:, D:2 * D])
                tw = [tok_ld, tok_gam]
                if wi == 1 or t > 0:
                    tw.append(pe.tok)
                p.op("vector", tern, waits=tw, incs=[(dve, 1)])
                tern_toks.append(dve.tok)
                i += 1
            tok_tb = dve.tok
            if wi == 1:
                p.op("vector", lambda e, t=t: e.tensor_reduce(
                    S_sb[:, t:t + 1], tb[:, 0:4 * D], axis=AX.X, op=ALU.add),
                    waits=[tok_tb], incs=[(dve, 1)])
            for r in range((nch + 3) // 4):
                def tp4(e, r=r, nch=nch):
                    ins = None
                    for k in range(4 * r, min(4 * r + 4, nch)):
                        ins = e.transpose(ps["BF"][:, (k % 4) * P:(k % 4 + 1) * P],
                                          tb[:, k * P:(k + 1) * P], idb)
                    return ins
                w = [tok_tb, tok_c]
                if prev_ev is not None:
                    w.append(prev_ev)
                p.op("tensor", tp4, waits=w, incs=[(pe, 1)])
                tok_tp = pe.tok

                def ev(e, r=r):
                    return e.tensor_copy(tt[:, (4 * r % NF) * P:((4 * r % NF) + 4) * P],
                                         ps["BF"][:, 0:4 * P])
                evw = [tok_tp]
                if prev_st is not None and (4 * r) % NF == 0:
                    evw.append(prev_st)
                p.op("vector", ev, waits=evw, incs=[(dve, 1)])
                prev_ev = dve.tok
                if (4 * r + 4) % NF == 0:
                    seg = (4 * r + 4) // NF - 1
                    dst = io["tfc"][t] if wi == 0 else io["tpr"][t, seg]
                    p.dma("gpsimd", dst, tt[:, 0:D], stw, waits=[prev_ev])
                    prev_st = stw.tok
    tok_wprep = stw.tok
    ar.pop()
    arh.pop()

    p.fence()
    ar.push()
    arh.push()
    xnh = arh.alloc(NF * 512, "xnh")
    xnl = arh.alloc(NF * 512, "xnl")
    xnt = ar.alloc(D, "xnt")
    sq = ar.alloc(D, "sq")
    gscr = ar.alloc(512, "gscr")
    gsq = ar.alloc(512, "gsq")
    hhb = arh.alloc(512, "hhb")
    hlb = arh.alloc(512, "hlb")
    tts = [arh.alloc(D, "tts0"), arh.alloc(D, "tts1")]
    ldtp = [p.sem("ldt0"), p.sem("ldt1")]

    def z(e):
        e.memset(srow[0:1, :], 0.0)
        return e.memset(srow2[0:1, :], 0.0)
    p.op("vector", z, incs=[(dve, 1)])

    for th in range(2):
        if th > 0:
            p.fence()
        for jj in range(4):
            j = th * 4 + jj
            tok_ln = emit_ln(p, XS[:, j * D:(j + 1) * D], xnt, sq, stats, 1e-6, D,
                             wait_y=[tok_x] if j == 0 else [])
            for r in range(2):
                def tp8(e, r=r):
                    ins = None
                    for k in range(8 * r, 8 * r + 8):
                        ins = e.transpose(ps["A"][:, (k % 8) * P:(k % 8 + 1) * P],
                                          xnt[:, k * P:(k + 1) * P], idf)
                    return ins
                p.op("tensor", tp8, waits=[tok_ln, tok_c, (dve, dve.n)],
                     incs=[(pe, 1)])
                tok_tp = pe.tok

                def split(e, jj=jj, r=r):
                    ins = None
                    for k in range(8 * r, 8 * r + 8):
                        dst = slice(k * 512 + jj * P, k * 512 + (jj + 1) * P)
                        e.tensor_copy(xnh[:, dst],
                                      ps["A"][:, (k % 8) * P:(k % 8 + 1) * P])
                    for k in range(8 * r, 8 * r + 8):
                        dst = slice(k * 512 + jj * P, k * 512 + (jj + 1) * P)
                        ins = e.tensor_sub(xnl[:, dst],
                                           ps["A"][:, (k % 8) * P:(k % 8 + 1) * P],
                                           xnh[:, dst])
                    return ins
                p.op("vector", split, waits=[tok_tp], incs=[(dve, 1)])
        tok_xn = dve.tok

        pe_fc = []
        ev_fc = []
        for ob in range(NOB):
            w = [pe_fc[ob - 2]] if ob >= 2 else []
            if th == 0 and ob <= 1:
                w.append(tok_wprep)
            p.dma("sync", tts[ob % 2], io["tfc"][ob], ldtp[ob % 2], waits=w)
            tok_ldt = ldtp[ob % 2].tok

            def fcmm(e, ob=ob):
                ins = None
                dstp = ps["A"][:, (ob % 2) * 512:(ob % 2) * 512 + 512]
                for dc in range(NF):
                    for hl, xsrc in ((0, xnh), (1, xnl)):
                        ins = e.matmul(dstp,
                                       tts[ob % 2][:, dc * P:(dc + 1) * P],
                                       xsrc[:, dc * 512:(dc + 1) * 512],
                                       start=(dc == 0 and hl == 0),
                                       stop=(dc == NF - 1 and hl == 1))
                return ins
            wmm = [tok_ldt, tok_xn, tok_gam]
            if ob >= 2:
                wmm.append(ev_fc[ob - 2])
            p.op("tensor", fcmm, waits=wmm, incs=[(pe, 1)])
            pe_fc.append(pe.tok)

            p.chain("scalar", [
                (lambda e, ob=ob: e.activation(
                    gscr, ps["A"][:, (ob % 2) * 512:(ob % 2) * 512 + 512],
                    AF.Gelu, scale=gbuf[:, 0:1])),
                lambda e: e.activation(gsq, gscr, AF.Square),
            ], waits=[pe_fc[ob], (dve, dve.n)], incs=[(act, 1)])
            tok_gel = act.tok

            p.chain("vector", [
                lambda e: e.tensor_copy(hhb, gscr),
                lambda e: e.tensor_sub(hlb, gscr, hhb),
            ], waits=[tok_gel, (sth, sth.n)], incs=[(dve, 1)])
            tok_split = dve.tok

            def stmm(e):
                e.matmul(ps["Dk"][0:1, 0:512], ones[:, 1:2], gscr,
                         start=True, stop=True)
                return e.matmul(ps["C"][0:1, 0:512], ones[:, 1:2], gsq,
                                start=True, stop=True)
            p.op("tensor", stmm, waits=[tok_gel], incs=[(pe, 1)])
            tok_st = pe.tok

            def stacc(e, th=th):
                sl = slice(th * 512, (th + 1) * 512)
                e.tensor_add(srow[0:1, sl], srow[0:1, sl], ps["Dk"][0:1, 0:512])
                return e.tensor_add(srow2[0:1, sl], srow2[0:1, sl],
                                    ps["C"][0:1, 0:512])
            p.op("vector", stacc, waits=[tok_st], incs=[(dve, 1)])
            ev_fc.append(dve.tok)

            p.dma("gpsimd", io["hh"][ob][:, th * 512:(th + 1) * 512], hhb, sth,
                  waits=[tok_split])
            p.dma("gpsimd", io["hl"][ob][:, th * 512:(th + 1) * 512], hlb, sth)
    tok_fc_done = dve.tok
    tok_fc_store = sth.tok
    ar.pop()
    arh.pop()

    rrow = ar.alloc(TOK, "rrow")
    mrow = ar.alloc(TOK, "mrow")

    p.chain("vector", [
        lambda e: e.tensor_scalar_mul(mrow[0:1, :], srow[0:1, :], 1.0 / OD),
        lambda e: e.tensor_mul(rrow[0:1, :], mrow[0:1, :], mrow[0:1, :]),
        lambda e: e.scalar_tensor_tensor(rrow[0:1, :], srow2[0:1, :], 1.0 / OD,
                                         rrow[0:1, :], ALU.mult, ALU.subtract),
        lambda e: e.tensor_scalar_add(rrow[0:1, :], rrow[0:1, :], 1e-6),
        lambda e: e.reciprocal(rrow[0:1, :], rrow[0:1, :]),
    ], waits=[tok_fc_done], incs=[(dve, 1)])
    p.op("scalar", lambda e: e.activation(rrow[0:1, :], rrow[0:1, :], AF.Sqrt),
         waits=[dve.tok], incs=[(act, 1)])

    p.chain("vector", [
        lambda e: e.tensor_single_scalar(rrow[0:1, :], rrow[0:1, :],
                                         gbuf[0:1, 3:4], ALU.mult),
        lambda e: e.tensor_mul(mrow[0:1, :], mrow[0:1, :], rrow[0:1, :]),
    ], waits=[act.tok], incs=[(dve, 1)])
    tok_rows = dve.tok

    p.fence()
    prs = [arh.alloc(4 * 8 * P, "prs0"), arh.alloc(4 * 8 * P, "prs1")]
    hbuf = [arh.alloc(1024, "hb0"), arh.alloc(1024, "hb1")]
    sq = ar.alloc(D, "sq2")
    ldhp = [p.sem("ldh0"), p.sem("ldh1")]
    rb = ar.alloc(512, "rb")
    mrb = ar.alloc(512, "mrb")
    yfm = ar.alloc(NF * 512, "yfm")
    ysum = ar.alloc(D, "ysum")
    psall = [ps["A"][:, 0:512], ps["A"][:, 512:1024],
             ps["B"][:, 0:512], ps["B"][:, 512:1024]]
    last_grp_pe = pe.tok
    for th in range(2):
        def bcast(e, th=th):
            e.matmul(ps["C"][:, 0:512], ones[0:1, 0:P],
                     rrow[0:1, th * 512:(th + 1) * 512], start=True, stop=True)
            return e.matmul(ps["C"][:, 512:1024], ones[0:1, 0:P],
                            mrow[0:1, th * 512:(th + 1) * 512], start=True, stop=True)
        p.op("tensor", bcast, waits=[tok_rows, (dve, dve.n)], incs=[(pe, 1)])

        def evb(e):
            e.tensor_copy(rb, ps["C"][:, 0:512])
            return e.tensor_copy(mrb, ps["C"][:, 512:1024])
        p.op("vector", evb, waits=[pe.tok], incs=[(dve, 1)])
        tok_rb = dve.tok
        for ybg in range(4):
            mm_toks = []
            for ob in range(NOB):
                og = ob // 8
                if ob % 8 == 0:
                    w = [mm_toks[(og - 2) * 8 + 7]] if og >= 2 else [last_grp_pe]
                    if th == 0 and ybg == 0 and og <= 1:
                        w.append(tok_wprep)
                    src = io["tpr_t"][4 * ybg:4 * ybg + 4, og // 2, :,
                                      (og % 2) * 1024:(og % 2) * 1024 + 1024]
                    p.op("sync", (lambda e, s=src, b=og % 2:
                                  e.dma_start(out=prs[b].rearrange("p (y k) -> p y k", y=4),
                                              in_=s.rearrange("y p c -> p y c"))),
                         waits=w, incs=[(ldtp[og % 2], 16)])
                tok_slab = ldtp[og % 2].tok
                hw = [mm_toks[ob - 2]] if ob >= 2 else [last_grp_pe]
                if th == 0 and ybg == 0 and ob <= 1:
                    hw.append(tok_fc_store)
                p.dma("sync", hbuf[ob % 2][:, 0:512],
                      io["hh"][ob][:, th * 512:(th + 1) * 512], ldhp[ob % 2], waits=hw)
                p.dma("sync", hbuf[ob % 2][:, 512:1024],
                      io["hl"][ob][:, th * 512:(th + 1) * 512], ldhp[ob % 2])
                tok_h = ldhp[ob % 2].tok

                def prmm(e, ob=ob, og=og):
                    ins = None
                    for yi in range(4):
                        for hl in range(2):
                            ins = e.matmul(
                                psall[yi],
                                prs[og % 2][:, (yi * 8 + ob % 8) * P:
                                            (yi * 8 + ob % 8 + 1) * P],
                                hbuf[ob % 2][:, hl * 512:(hl + 1) * 512],
                                start=(ob == 0 and hl == 0),
                                stop=(ob == NOB - 1 and hl == 1))
                    return ins
                wmm = [tok_slab, tok_h]
                if ybg == 0 and ob == 0:
                    wmm.append(tok_rb)
                p.op("tensor", prmm, waits=wmm, incs=[(pe, 1)])
                mm_toks.append(pe.tok)
            last_grp_pe = pe.tok

            def fold_a(e, ybg=ybg):
                ins = None
                for yi in range(4):
                    yb = 4 * ybg + yi
                    e.tensor_mul(yfm[:, yb * 512:(yb + 1) * 512], psall[yi], rb)
                    ins = e.tensor_single_scalar(ysum[:, yi * 512:(yi + 1) * 512],
                                                 mrb, S_sb[:, yb:yb + 1], ALU.mult)
                return ins

            def fold_b(e, ybg=ybg):
                ins = None
                for yi in range(4):
                    yb = 4 * ybg + yi
                    dst = yfm[:, yb * 512:(yb + 1) * 512]
                    ins = e.tensor_sub(dst, dst, ysum[:, yi * 512:(yi + 1) * 512])
                return ins
            p.chain("vector", [fold_a, fold_b], waits=[last_grp_pe], incs=[(dve, 1)])
            last_grp_pe = None
            last_grp_pe = mm_toks[-1]
        tok_fold = dve.tok
        for tt_ in range(4):
            j = th * 4 + tt_
            for r in range(2):
                def tpy(e, tt_=tt_, r=r):
                    ins = None
                    for k in range(8 * r, 8 * r + 8):
                        ins = e.transpose(ps["A"][:, (k % 8) * P:(k % 8 + 1) * P],
                                          yfm[:, k * 512 + tt_ * P:
                                              k * 512 + (tt_ + 1) * P], idf)
                    return ins
                p.op("tensor", tpy, waits=[tok_fold, (dve, dve.n)], incs=[(pe, 1)])
                tok_tp = pe.tok

                def evy(e, j=j, r=r):
                    ins = None
                    for k in range(8 * r, 8 * r + 8):
                        ins = e.scalar_tensor_tensor(
                            ysum[:, k * P:(k + 1) * P],
                            ps["A"][:, (k % 8) * P:(k % 8 + 1) * P], 0.0,
                            XS[:, j * D + k * P: j * D + (k + 1) * P],
                            ALU.add, ALU.add)
                    return ins
                p.op("vector", evy, waits=[tok_tp], incs=[(dve, 1)])
            tok_y = dve.tok
            emit_ln(p, ysum, XS[:, j * D:(j + 1) * D], sq, stats, 1e-5, D,
                    wait_y=[tok_y])
        last_grp_pe = pe.tok

    for j in range(TPC):
        p.dma("gpsimd", io["xout"][j * P:(j + 1) * P, :], XS[:, j * D:(j + 1) * D],
              sto, waits=[(dve, dve.n)])
    p.op("gpsimd", lambda e: e.engine_nop(), waits=[sto.tok])


def build_l0_ffn():
    nc = bass.Bass("TRN2", target_bir_lowering=False)
    io = {}
    io["xin"] = nc.dram_tensor("xin", [TOK, D], F32, kind="ExternalInput").ap()
    io["wfc"] = nc.dram_tensor("wfc", [OD, D], F32, kind="ExternalInput").ap()
    io["wpr"] = nc.dram_tensor("wpr", [D, OD], F32, kind="ExternalInput").ap()
    io["ident"] = nc.dram_tensor("ident", [P, P], F32, kind="ExternalInput").ap()
    io["identb"] = nc.dram_tensor("identb", [P, P], BF16, kind="ExternalInput").ap()
    io["ones"] = nc.dram_tensor("ones", [P, P], F32, kind="ExternalInput").ap()
    io["xout"] = nc.dram_tensor("xout", [TOK, D], F32, kind="ExternalOutput").ap()
    hh = nc.dram_tensor("hh_scr", [NOB, P, TOK], BF16).ap()
    hl = nc.dram_tensor("hl_scr", [NOB, P, TOK], BF16).ap()
    tfc = nc.dram_tensor("tfc_scr", [NOB, P, NF * P], BF16).ap()
    tpr = nc.dram_tensor("tpr_scr", [NF, 4, P, NF * P], BF16).ap()
    io["hh"] = [hh[ob] for ob in range(NOB)]
    io["hl"] = [hl[ob] for ob in range(NOB)]
    io["tfc"] = [tfc[ob] for ob in range(NOB)]
    io["tpr"] = tpr
    io["tpr_t"] = tpr

    est = contextlib.ExitStack()
    with est:
        SB = est.enter_context(nc.sbuf_tensor("SB", [P, 34816], F32))
        SH = est.enter_context(nc.sbuf_tensor("SH", [P, 26000], BF16))
        ps = {}
        ps["A"] = est.enter_context(nc.psum_tensor("psA", [P, 1024], F32))
        ps["B"] = est.enter_context(nc.psum_tensor("psB", [P, 1024], F32))
        ps["C"] = est.enter_context(nc.psum_tensor("psC", [P, 1024], F32))
        ps["Dk"] = est.enter_context(nc.psum_tensor("psD", [P, 512], F32))
        ps["BF"] = est.enter_context(nc.psum_tensor("psBF", [P, 512], BF16))
        ar = Arena(SB, 34816, "f32")
        arh = Arena(SH, 26000, "bf16")
        p = Prog(nc, est)
        _emit_l0_ffn(p, ar, arh, io, ps)
        p.finalize()
    return nc


def shard_rows(x2d):
    out = []
    xt = np.asarray(x2d).reshape(B, NT, P, D)
    for c in range(NCORES):
        b, g = c // 4, c % 4
        tiles = [xt[b, 4 * j + g] for j in range(TPC)]
        out.append(np.ascontiguousarray(np.concatenate(tiles, axis=0)))
    return out


def unshard_rows(slices):
    x = np.zeros((B, NT, P, D), dtype=np.float32)
    for c in range(NCORES):
        b, g = c // 4, c % 4
        s = np.asarray(slices[c]).reshape(TPC, P, D)
        for j in range(TPC):
            x[b, 4 * j + g] = s[j]
    return x.reshape(B * SEQ, D)


def run_l0_ffn(x_slices, w_fc, w_proj):
    import ml_dtypes
    nc = build_l0_ffn()
    consts = {
        "ident": np.eye(P, dtype=np.float32),
        "identb": np.eye(P, dtype=ml_dtypes.bfloat16),
        "ones": np.ones((P, P), dtype=np.float32),
    }
    in_maps = []
    for c in range(NCORES):
        m = dict(consts)
        m["xin"] = np.ascontiguousarray(x_slices[c], dtype=np.float32)
        m["wfc"] = np.ascontiguousarray(w_fc, dtype=np.float32)
        m["wpr"] = np.ascontiguousarray(w_proj, dtype=np.float32)
        in_maps.append(m)
    res = run_bass_kernel_spmd(nc, in_maps, core_ids=list(range(NCORES)))
    return [r["xout"] for r in res.results]






def _emit_l0_attn(p, ar, io, ps, upto=None):
    ldc, ldx = p.sem("ldc"), p.sem("ldx")
    pe, dve, act = p.sem("pe"), p.sem("dve"), p.sem("act")
    stk, sto = p.sem("stk"), p.sem("sto")
    ldxp = [p.sem("ldxp0"), p.sem("ldxp1")]
    ldwp = [p.sem("ldwp0"), p.sem("ldwp1")]
    ldkp = [p.sem("ldkp0"), p.sem("ldkp1")]
    kv_of_slot = _kv_of_slot()

    XS = ar.alloc(TPC * D, "XS")
    CON = ar.alloc(3 * P, "consts")
    stats = ar.alloc(16, "stats")
    Pt_s, id_s, on_s = CON[:, 0:P], CON[:, P:2 * P], CON[:, 2 * P:3 * P]

    p.dma("sync", Pt_s, io["Pt"], ldc)
    p.dma("sync", id_s, io["ident"], ldc)
    p.dma("sync", on_s, io["ones"], ldc)
    for j in range(TPC):
        p.dma("sync", XS[:, j * D:(j + 1) * D], io["xown"][j * P:(j + 1) * P, :], ldx)
    tok_c = ldc.tok
    tok_x = ldx.tok

    ar.push()
    T1s = ar.alloc(SEQ, "T1")
    T2s = ar.alloc(SEQ, "T2")
    p.dma("sync", T1s, io["T1"], ldc)
    p.dma("sync", T2s, io["T2"], ldc)
    tok_c = ldc.tok
    wks = ar.alloc(NF * KVD, "wks")
    xb = [ar.alloc(D, "xb0"), ar.alloc(D, "xb1")]
    XF = ar.alloc(D, "XF")
    kraw = ar.alloc(KVD, "kraw")
    ktmp = ar.alloc(KVD, "ktmp")
    kf = ar.alloc(KVD, "kf")
    vb = ar.alloc(8 * 65, "vb")
    tp = ps["A"][:, 0:D]
    pk = ps["Bp"][:, 0:512]
    kws = ps["Bp"][:, 512:1024]

    vones = bass.AP(vb.tensor, vb.offset + 64, [[vb.tensor.shape[1], P], [65, 8], [1, 1]])
    p.op("vector", lambda e: e.memset(vones, 1.0), incs=[(dve, 1)])

    for pas in range(2):
        w_in = io["wk"] if pas == 0 else io["wv"]
        for fc in range(NF):
            p.dma("sync", wks[:, fc * KVD:(fc + 1) * KVD],
                  w_in[fc * P:(fc + 1) * P, :], ldwp[0],
                  waits=[pe.tok] if fc == 0 and pas == 1 else [])
        tok_w = ldwp[0].tok
        for t in range(NT):
            it = pas * NT + t
            lw = [pe.tok] if it >= 2 else []
            p.dma("sync", xb[it % 2], io["xfull"][t * P:(t + 1) * P, :],
                  ldxp[it % 2], waits=lw)
            tok_xb = ldxp[it % 2].tok

            def tps(e, b=it % 2):
                ins = None
                for k in range(NF):
                    ins = e.transpose(tp[:, k * P:(k + 1) * P],
                                      xb[b][:, k * P:(k + 1) * P], id_s)
                return ins
            p.op("tensor", tps, waits=[tok_xb, tok_c, (dve, dve.n)], incs=[(pe, 1)])
            p.op("vector", lambda e: e.tensor_copy(XF, tp), waits=[pe.tok],
                 incs=[(dve, 1)])
            tok_xf = dve.tok
            if pas == 0:
                def kmm(e):
                    ins = None
                    for kc in range(4):
                        for fc in range(NF):
                            ins = e.matmul(pk[:, kc * P:(kc + 1) * P],
                                           wks[:, fc * KVD + kc * P:
                                               fc * KVD + (kc + 1) * P],
                                           XF[:, fc * P:(fc + 1) * P],
                                           start=(fc == 0), stop=(fc == NF - 1))
                    return ins
                p.op("tensor", kmm, waits=[tok_xf, tok_w, (stk, stk.n)],
                     incs=[(pe, 1)])
                p.op("vector", lambda e: e.tensor_copy(kraw, pk), waits=[pe.tok],
                     incs=[(dve, 1)])
                p.op("tensor", lambda e: e.matmul(kws, Pt_s, kraw, start=True,
                                                  stop=True),
                     waits=[dve.tok], incs=[(pe, 1)])
                tok_kw = pe.tok

                def rope_a(e, t=t):
                    ins = None
                    for c in range(4):
                        csl = slice(c * P, (c + 1) * P)
                        tsl = slice(t * P, (t + 1) * P)
                        ins = e.tensor_mul(ktmp[:, csl], kraw[:, csl], T1s[:, tsl])
                    for c in range(4):
                        csl = slice(c * P, (c + 1) * P)
                        tsl = slice(t * P, (t + 1) * P)
                        ins = e.tensor_mul(kf[:, csl], kws[:, csl], T2s[:, tsl])
                    return ins

                def rope_b(e):
                    ins = None
                    for c in range(4):
                        csl = slice(c * P, (c + 1) * P)
                        ins = e.tensor_add(kf[:, csl], kf[:, csl], ktmp[:, csl])
                    return ins
                p.chain("vector", [rope_a, rope_b], waits=[tok_kw], incs=[(dve, 1)])
                tok_kf = dve.tok
                for c in range(4):
                    p.dma("gpsimd", io["Kscr"][c * P:(c + 1) * P, t * P:(t + 1) * P],
                          kf[:, c * P:(c + 1) * P], stk,
                          waits=[tok_kf] if c == 0 else [])
            else:
                def vmm(e):
                    ins = None
                    for fc in range(NF):
                        ins = e.matmul(pk[:, 0:KVD],
                                       XF[:, fc * P:(fc + 1) * P],
                                       wks[:, fc * KVD:(fc + 1) * KVD],
                                       start=(fc == 0), stop=(fc == NF - 1))
                    return ins
                p.op("tensor", vmm, waits=[tok_xf, tok_w, (stk, stk.n)],
                     incs=[(pe, 1)])
                vspread = bass.AP(vb.tensor, vb.offset,
                                  [[vb.tensor.shape[1], P], [65, 8], [1, 64]])
                p.op("vector", lambda e: e.tensor_copy(vspread, pk),
                     waits=[pe.tok], incs=[(dve, 1)])
                p.dma("gpsimd", io["Vscr"][t * P:(t + 1) * P, :], vb[:, 0:520], stk,
                      waits=[dve.tok])
    ar.pop()
    tok_kv_done = stk.tok
    if upto == "kv":
        p.dma("gpsimd", io["xout"].rearrange("a d -> (a d)").rearrange("(r c) -> r c", r=KVD),
              io["Kscr"], sto, waits=[tok_kv_done])
        p.op("gpsimd", lambda e: e.engine_nop(), waits=[sto.tok])
        return
    QA = ar.alloc(TPC * D, "QA")

    p.fence()
    ar.push()
    wqb = [ar.alloc(D, "wqb0"), ar.alloc(D, "wqb1")]
    XFa = ar.alloc(D, "XFa")
    XFb = ar.alloc(D, "XFb")
    for pr in range(4):
        for half in range(2):
            tj = 2 * pr + half
            dst = ps["A"] if half == 0 else ps["Bp"]

            def tpq(e, tj=tj, dst=dst):
                ins = None
                for k in range(NF):
                    ins = e.transpose(dst[:, k * P:(k + 1) * P],
                                      XS[:, tj * D + k * P: tj * D + (k + 1) * P],
                                      id_s)
                return ins
            p.op("tensor", tpq, waits=[tok_x, tok_c, (dve, dve.n)], incs=[(pe, 1)])
        tok_tp = pe.tok

        def evq(e):
            e.tensor_copy(XFa, ps["A"][:, 0:D])
            return e.tensor_copy(XFb, ps["Bp"][:, 0:D])
        p.op("vector", evq, waits=[tok_tp], incs=[(dve, 1)])
        tok_xf = dve.tok
        mm_toks = []
        for fc in range(NF):
            i = pr * NF + fc
            w = [mm_toks[fc - 2]] if fc >= 2 else ([pe.tok] if fc < 2 else [])
            p.dma("sync", wqb[fc % 2], io["wq"][fc * P:(fc + 1) * P, :],
                  ldwp[fc % 2], waits=w)
            tok_wq = ldwp[fc % 2].tok

            def qmm(e, fc=fc):
                ins = None
                for half in range(2):
                    src = XFa if half == 0 else XFb
                    dst = ps["A"] if half == 0 else ps["Bp"]
                    for nb in range(4):
                        ins = e.matmul(dst[:, nb * 512:(nb + 1) * 512],
                                       src[:, fc * P:(fc + 1) * P],
                                       wqb[fc % 2][:, nb * 512:(nb + 1) * 512],
                                       start=(fc == 0), stop=(fc == NF - 1))
                return ins
            p.op("tensor", qmm, waits=[tok_wq, tok_xf], incs=[(pe, 1)])
            mm_toks.append(pe.tok)

        def evq2(e, pr=pr):
            e.tensor_copy(QA[:, (2 * pr) * D:(2 * pr + 1) * D], ps["A"][:, 0:D])
            return e.tensor_copy(QA[:, (2 * pr + 1) * D:(2 * pr + 2) * D],
                                 ps["Bp"][:, 0:D])
        p.op("vector", evq2, waits=[mm_toks[-1]], incs=[(dve, 1)])
    ar.pop()
    if upto == "q":
        p.fence()
        for j in range(TPC):
            p.dma("gpsimd", io["xout"][j * P:(j + 1) * P, :],
                  QA[:, j * D:(j + 1) * D], sto, waits=[(dve, dve.n)])
        p.op("gpsimd", lambda e: e.engine_nop(), waits=[sto.tok])
        return

    p.fence()
    ar.push()
    Tq1 = ar.alloc(P, "Tq1")
    Tq2 = ar.alloc(P, "Tq2")
    ldtq = p.sem("ldtq")
    qfm = ar.alloc(D, "qfm")
    stq = p.sem("stq")
    ldq2 = p.sem("ldq2")
    prev_av_last = None
    tok_avso_prev = None
    flash_tiles = [0] if (upto or "").startswith("flash1") else list(range(TPC))
    qsw = ar.alloc(D, "qsw")
    stage = qsw
    qfr = ar.alloc(D, "qfr")
    qf2 = ar.alloc(4096, "qf2")
    WT = [ar.alloc(1024, "WT0"), ar.alloc(1024, "WT1")]
    kb = [ar.alloc(8 * P, "kb0"), ar.alloc(8 * P, "kb1")]
    vbf = [ar.alloc(544, "vb0"), ar.alloc(544, "vb1")]
    avso = ar.alloc(1024, "avso")
    rsum = ar.alloc(1024, "rsum")
    dmb2 = [ar.alloc(1024, "dm0"), ar.alloc(1024, "dm1")]
    ldmp2 = [p.sem("ldm0"), p.sem("ldm1")]
    spsum = [ps["A"][:, 0:1024], ps["A"][:, 1024:2048]]
    avp = ps["Bp"][:, 0:1024]
    scr = ps["Bp"][:, 1024:2048]
    ldmp = p.sem("ldmp")

    for j in flash_tiles:
        nk = 4 * j + 4
        p.dma("sync", Tq1, io["Tq1"][:, j * P:(j + 1) * P], ldtq,
              waits=[(dve, dve.n)])
        p.dma("sync", Tq2, io["Tq2"][:, j * P:(j + 1) * P], ldtq)
        tok_tq = ldtq.tok
        for r in range(2):
            def tpq(e, j=j, r=r):
                ins = None
                for k in range(8 * r, 8 * r + 8):
                    ins = e.transpose(scr[:, (k % 8) * P:(k % 8 + 1) * P],
                                      QA[:, j * D + k * P: j * D + (k + 1) * P],
                                      id_s)
                return ins
            p.op("tensor", tpq, waits=[(dve, dve.n), tok_c], incs=[(pe, 1)])

            def evt(e, r=r):
                return e.tensor_copy(qfm[:, 8 * r * P:(8 * r + 8) * P], scr)
            p.op("vector", evt, waits=[pe.tok], incs=[(dve, 1)])
        tok_qf = dve.tok
        def qswmm(e):
            ins = None
            for nb in range(4):
                ins = e.matmul(scr[:, (nb % 2) * 512:(nb % 2) * 512 + 512]
                               if False else scr[:, 0:512],
                               Pt_s, qfm[:, nb * 512:(nb + 1) * 512],
                               start=True, stop=True)
            return ins
        for half in range(2):
            def qsw_mm(e, half=half):
                ins = None
                for nb in range(2):
                    ins = e.matmul(scr[:, nb * 512:(nb + 1) * 512], Pt_s,
                                   qfm[:, (2 * half + nb) * 512:
                                       (2 * half + nb + 1) * 512],
                                   start=True, stop=True)
                return ins
            p.op("tensor", qsw_mm, waits=[tok_qf] + ([dve.tok] if half else []),
                 incs=[(pe, 1)])
            p.op("vector", (lambda e, half=half: e.tensor_copy(
                qsw[:, half * 1024:(half + 1) * 1024], scr)),
                waits=[pe.tok] + ([stq.tok] if half == 0 else []), incs=[(dve, 1)])

        def rope_qa(e):
            ins = None
            for c in range(NF):
                csl = slice(c * P, (c + 1) * P)
                ins = e.tensor_mul(qfr[:, csl], qfm[:, csl], Tq1[:, 0:P])
            return ins

        def rope_qb(e):
            ins = None
            for c in range(NF):
                csl = slice(c * P, (c + 1) * P)
                ins = e.scalar_tensor_tensor(qfm[:, csl], qsw[:, csl], 0.0,
                                             Tq2[:, 0:P], ALU.add, ALU.mult)
            return ins

        def rope_qc(e):
            ins = None
            for c in range(NF):
                csl = slice(c * P, (c + 1) * P)
                ins = e.tensor_add(qfr[:, csl], qfr[:, csl], qfm[:, csl])
            return ins
        p.chain("vector", [rope_qa, rope_qb, rope_qc], waits=[dve.tok, tok_tq],
                incs=[(dve, 1)])
        tok_rope = dve.tok
        p.dma("gpsimd", qf2[0:64, :].rearrange("p (t n) -> p t n", n=2 * P)[:, :, 0:P],
              qfr[0:64, :].rearrange("p (t n) -> p t n", n=P), ldq2,
              waits=[tok_rope])
        p.dma("gpsimd", qf2[0:64, :].rearrange("p (t n) -> p t n", n=2 * P)[:, :, P:2 * P],
              qfr[64:128, :].rearrange("p (t n) -> p t n", n=P), ldq2)
        tok_q = ldq2.tok

        tok_stage_free = stq.tok
        prev_mask_last = None
        for oc in range(4):
            exp_toks = {}
            av_toks = {}
            sc_toks = {}
            mask_toks = {}
            for Tk in range(nk):
                lw = [av_toks[Tk - 2]] if Tk >= 2 else ([prev_av_last] if prev_av_last else [])
                p.dma("sync", kb[Tk % 2][0:64, :].rearrange("p (h n) -> p h n", n=P),
                      io["Kscr"].rearrange("(h p) n -> p h n", p=64)[:, :, Tk * P:(Tk + 1) * P],
                      ldkp[Tk % 2], waits=lw + ([tok_kv_done] if j == 0 and oc == 0 and Tk <= 1 else []))
                p.dma("sync", vbf[Tk % 2][:, 0:520],
                      io["Vscr"][Tk * P:(Tk + 1) * P, :], ldkp[Tk % 2])
                tok_kvl = ldkp[Tk % 2].tok

                if upto in ("flash_ld", "flash1_qprep"):
                    sc_toks[Tk] = tok_kvl
                    exp_toks[Tk] = tok_kvl
                    av_toks[Tk] = tok_kvl
                    continue

                def smm(e, oc=oc, Tk=Tk):
                    ins = None
                    for si in range(8):
                        s = 8 * oc + si
                        jkv = kv_of_slot[s]
                        lhs = kb[Tk % 2][0:64, jkv * P:(jkv + 1) * P]
                        rhs = qf2[0:64, s * P:(s + 1) * P]
                        ins = e.matmul(spsum[Tk % 2][:, si * P:(si + 1) * P],
                                       lhs, rhs, start=(si % 4 == 0),
                                       stop=(si % 4 == 3))
                    return ins
                sw = [tok_kvl, tok_q]
                if Tk >= 2:
                    sw.append(exp_toks[Tk - 2])
                p.op("tensor", smm, waits=sw, incs=[(pe, 1)])
                sc_toks[Tk] = pe.tok

                if upto == "flash1_sc":
                    exp_toks[Tk] = sc_toks[Tk]
                    av_toks[Tk] = sc_toks[Tk]
                    continue
                ew = [sc_toks[Tk]]
                if Tk >= 2:
                    ew.append(av_toks[Tk - 2])
                p.op("scalar", (lambda e, Tk=Tk: e.activation(
                    WT[Tk % 2], spsum[Tk % 2], AF.Exp, scale=0.125)),
                    waits=ew, incs=[(act, 1)])
                exp_toks[Tk] = act.tok
                tok_wt = exp_toks[Tk]
                if Tk >= 4 * j:
                    dt = Tk - 4 * j
                    mw = [mask_toks[dt - 2]] if dt >= 2 else [prev_mask_last]
                    p.dma("sync", dmb2[dt % 2],
                          io["dmask"][j * P:(j + 1) * P, dt * 1024:(dt + 1) * 1024],
                          ldmp2[dt % 2], waits=mw)
                    tok_dml = ldmp2[dt % 2].tok
                    p.op("vector", (lambda e, Tk=Tk, dt=dt: e.tensor_mul(
                        WT[Tk % 2], WT[Tk % 2], dmb2[dt % 2])),
                        waits=[tok_wt, tok_dml], incs=[(dve, 1)])
                    mask_toks[dt] = dve.tok
                    tok_wt = dve.tok

                def avmm(e, oc=oc, Tk=Tk, nk=nk):
                    ins = None
                    for si in range(8):
                        s = 8 * oc + si
                        jkv = kv_of_slot[s]
                        ins = e.matmul(avp[0:65, si * P:(si + 1) * P],
                                       vbf[Tk % 2][:, 65 * jkv:65 * jkv + 65],
                                       WT[Tk % 2][:, si * P:(si + 1) * P],
                                       start=(Tk == 0 and si % 4 == 0),
                                       stop=(Tk == nk - 1 and si % 4 == 3))
                    return ins
                if upto == "flash_noav":
                    av_toks[Tk] = exp_toks[Tk]
                else:
                    aw = [tok_wt]
                    if Tk == 0:
                        aw.append(tok_avso_prev)
                    p.op("tensor", avmm, waits=aw, incs=[(pe, 1)])
                    av_toks[Tk] = pe.tok
            if upto in ("flash_ld", "flash1_qprep", "flash1_sc", "flash1_exp"):
                tok_avso_prev = av_toks[nk - 1]
                prev_av_last = av_toks[nk - 1]
                continue
            if upto in ("flash_noepi", "flash_noav"):
                if upto == "flash_noepi":
                    p.op("vector", lambda e: e.tensor_copy(avso[0:65, :], avp[0:65, :]),
                         waits=[av_toks[nk - 1]], incs=[(dve, 1)])
                tok_avso_prev = (p.sems["dve"], p.sems["dve"].n)
                prev_av_last = av_toks[nk - 1]
                continue
            p.op("vector", lambda e: e.tensor_copy(avso[0:65, :], avp[0:65, :]),
                 waits=[av_toks[nk - 1]], incs=[(dve, 1)])
            tok_avso = dve.tok
            tok_avso_prev = tok_avso
            p.op("vector", lambda e: e.reciprocal(rsum[64:65, 0:1024],
                                                  avso[64:65, 0:1024]),
                 waits=[tok_avso], incs=[(dve, 1)])
            p.dma("gpsimd", rsum[0:1, 0:1024], rsum[64:65, 0:1024], stq,
                  waits=[(dve, dve.n)])
            tok_rs = stq.tok

            def bcasts(e):
                ins = None
                for si in range(8):
                    ins = e.matmul(scr[0:64, si * P:(si + 1) * P],
                                   on_s[0:1, 0:64],
                                   rsum[0:1, si * P:(si + 1) * P],
                                   start=(si % 4 == 0), stop=(si % 4 == 3))
                return ins
            p.op("tensor", bcasts, waits=[tok_rs, tok_c], incs=[(pe, 1)])
            tok_bc = pe.tok

            def norm_mults(e, oc=oc, j=j):
                ins = None
                for si in range(8):
                    s = 8 * oc + si
                    if s % 2 == 0:
                        dst = QA[0:64, j * D + (s // 2) * P: j * D + (s // 2 + 1) * P]
                    else:
                        dst = stage[0:64, (s // 2) * P:(s // 2 + 1) * P]
                    ins = e.tensor_mul(dst, avso[0:64, si * P:(si + 1) * P],
                                       scr[0:64, si * P:(si + 1) * P])
                return ins
            p.op("vector", norm_mults, waits=[tok_bc, tok_stage_free],
                 incs=[(dve, 1)])
            prev_av_last = av_toks[nk - 1]
            prev_mask_last = mask_toks.get(3) or mask_toks.get(max(mask_toks) if mask_toks else 0)
        p.dma("gpsimd", QA[64:128, j * D:(j + 1) * D], stage[0:64, :], stq,
              waits=[(dve, dve.n)])
    ar.pop()
    if upto is not None and upto.startswith("flash"):
        p.fence()
        for j in range(TPC):
            p.dma("gpsimd", io["xout"][j * P:(j + 1) * P, :],
                  QA[:, j * D:(j + 1) * D], sto, waits=[(dve, dve.n)])
        p.op("gpsimd", lambda e: e.engine_nop(), waits=[sto.tok])
        return

    p.fence()
    ar.push()
    wob = [ar.alloc(D, "wob0"), ar.alloc(D, "wob1")]
    ysum = ar.alloc(D, "ysum")
    sq = ar.alloc(D, "sq")
    prev_pair_tok = pe.tok
    for pr in range(4):
        mm_toks = []
        for fc in range(NF):
            w = [mm_toks[fc - 2]] if fc >= 2 else [prev_pair_tok]
            p.dma("sync", wob[fc % 2], io["wo"][fc * P:(fc + 1) * P, :],
                  ldwp[fc % 2], waits=w)
            tok_w = ldwp[fc % 2].tok

            def wmm(e, pr=pr, fc=fc):
                ins = None
                for half in range(2):
                    tj = 2 * pr + half
                    dst = ps["A"] if half == 0 else ps["Bp"]
                    for nb in range(4):
                        ins = e.matmul(dst[:, nb * 512:(nb + 1) * 512],
                                       QA[:, tj * D + fc * P: tj * D + (fc + 1) * P],
                                       wob[fc % 2][:, nb * 512:(nb + 1) * 512],
                                       start=(fc == 0), stop=(fc == NF - 1))
                return ins
            ww = [tok_w]
            if pr > 0 and fc == 0:
                ww.append((dve, dve.n))
            p.op("tensor", wmm, waits=ww, incs=[(pe, 1)])
            mm_toks.append(pe.tok)
        for half in range(2):
            tj = 2 * pr + half
            src_ps = (ps["A"] if half == 0 else ps["Bp"])[:, 0:D]
            p.op("vector", (lambda e, tj=tj, src_ps=src_ps: e.scalar_tensor_tensor(
                ysum, src_ps, 0.0, XS[:, tj * D:(tj + 1) * D], ALU.add, ALU.add)),
                waits=[mm_toks[-1], (act, act.n)], incs=[(dve, 1)])
            emit_ln(p, ysum, XS[:, tj * D:(tj + 1) * D], sq, stats, 1e-5, D,
                    wait_y=[dve.tok])
        prev_pair_tok = mm_toks[-1]
    ar.pop()

    for j in range(TPC):
        p.dma("gpsimd", io["xout"][j * P:(j + 1) * P, :], XS[:, j * D:(j + 1) * D],
              sto, waits=[(dve, dve.n)])
    p.op("gpsimd", lambda e: e.engine_nop(), waits=[sto.tok])


def build_l0_attn(upto=None):
    nc = bass.Bass("TRN2", target_bir_lowering=False)
    io = {}
    io["xfull"] = nc.dram_tensor("xfull", [SEQ, D], F32, kind="ExternalInput").ap()
    io["xown"] = nc.dram_tensor("xown", [TOK, D], F32, kind="ExternalInput").ap()
    io["wq"] = nc.dram_tensor("wq", [D, D], F32, kind="ExternalInput").ap()
    io["wk"] = nc.dram_tensor("wk", [D, KVD], F32, kind="ExternalInput").ap()
    io["wv"] = nc.dram_tensor("wv", [D, KVD], F32, kind="ExternalInput").ap()
    io["wo"] = nc.dram_tensor("wo", [D, D], F32, kind="ExternalInput").ap()
    io["T1"] = nc.dram_tensor("T1", [P, SEQ], F32, kind="ExternalInput").ap()
    io["T2"] = nc.dram_tensor("T2", [P, SEQ], F32, kind="ExternalInput").ap()
    io["Tq1"] = nc.dram_tensor("Tq1", [P, TOK], F32, kind="ExternalInput").ap()
    io["Tq2"] = nc.dram_tensor("Tq2", [P, TOK], F32, kind="ExternalInput").ap()
    io["dmask"] = nc.dram_tensor("dmask", [TOK, 4096], F32, kind="ExternalInput").ap()
    io["Pt"] = nc.dram_tensor("Pt", [P, P], F32, kind="ExternalInput").ap()
    io["ident"] = nc.dram_tensor("ident", [P, P], F32, kind="ExternalInput").ap()
    io["ones"] = nc.dram_tensor("ones", [P, P], F32, kind="ExternalInput").ap()
    io["xout"] = nc.dram_tensor("xout", [TOK, D], F32, kind="ExternalOutput").ap()
    io["Kscr"] = nc.dram_tensor("Kscr", [KVD, SEQ], F32).ap()
    io["Vscr"] = nc.dram_tensor("Vscr", [SEQ, 8 * 65], F32).ap()

    est = contextlib.ExitStack()
    with est:
        SB = est.enter_context(nc.sbuf_tensor("SB", [P, 53100], F32))
        ps = {}
        ps["A"] = est.enter_context(nc.psum_tensor("psA", [P, 2048], F32))
        ps["Bp"] = est.enter_context(nc.psum_tensor("psB", [P, 2048], F32))
        ar = Arena(SB, 53100, "f32")
        p = Prog(nc, est)
        _emit_l0_attn(p, ar, io, ps, upto=upto)
        p.finalize()
    return nc


def l0_attn_inputs(x_np, wq, wk, wv, wo):
    wq_p, wk_p, wo_p = host_l0_perms(wq, wk, wo)
    T1, T2 = host_rope_tables(np.arange(SEQ))
    Pt = host_swap_pt()
    ident = np.eye(P, dtype=np.float32)
    ones = np.ones((P, P), dtype=np.float32)
    slices = shard_rows(np.asarray(x_np).reshape(-1, D))
    in_maps = []
    for c in range(NCORES):
        b, g = c // 4, c % 4
        cols = np.concatenate([np.arange((4 * j + g) * P, (4 * j + g + 1) * P)
                               for j in range(TPC)])
        Tq1, Tq2 = host_rope_tables(cols)
        m = dict(xfull=np.ascontiguousarray(x_np[b]), xown=slices[c],
                 wq=wq_p, wk=wk_p, wv=np.ascontiguousarray(wv), wo=wo_p,
                 T1=T1, T2=T2, Tq1=Tq1, Tq2=Tq2, dmask=host_dmask(g),
                 Pt=Pt, ident=ident, ones=ones)
        in_maps.append(m)
    return in_maps


def run_l0_attn(x_np, wq, wk, wv, wo):
    nc = build_l0_attn()
    in_maps = l0_attn_inputs(x_np, wq, wk, wv, wo)
    res = run_bass_kernel_spmd(nc, in_maps, core_ids=list(range(NCORES)))
    return [r["xout"] for r in res.results]



def _emit_l1a_part1(p, ar, io, ps):
    ldx = p.sem("ldx")
    pe, dve, act = p.sem("pe"), p.sem("dve"), p.sem("act")
    ldc = p.sem("ldc")
    ldwp = [p.sem("ldw0"), p.sem("ldw1")]
    sto = p.sem("sto")

    XS = ar.alloc(TPC * D, "XS")
    CON = ar.alloc(P, "ident")
    biases = ar.alloc(3 * D, "biases")
    for j in range(TPC):
        p.dma("sync", XS[:, j * D:(j + 1) * D], io["xin"][j * P:(j + 1) * P, :], ldx)
    p.dma("sync", CON, io["ident"], ldc)
    p.dma("sync", biases[:, 0:D], io["bqk"], ldc)
    p.dma("sync", biases[:, D:2 * D], io["bv"], ldc)
    p.dma("sync", biases[:, 2 * D:3 * D], io["bu"], ldc)
    tok_c = ldc.tok
    tok_x = ldx.tok

    wb = [ar.alloc(D, "wb0"), ar.alloc(D, "wb1")]
    XFa = ar.alloc(D, "XFa")
    XFb = ar.alloc(D, "XFb")
    featp = [ar.alloc(D, "fA"), ar.alloc(D, "fB")]
    vp = [ar.alloc(D, "vA"), ar.alloc(D, "vB")]
    scr = ar.alloc(D, "scr")
    kvacc = ar.alloc(D, "kvacc")
    p.op("vector", lambda e: e.memset(kvacc, 0.0))

    prev_pe = pe.tok
    for pr in range(4):
        for half in range(2):
            tj = 2 * pr + half
            dst = ps["A"] if half == 0 else ps["Bp"]
            p.op("tensor", (lambda e, tj=tj, dst=dst: [
                e.transpose(dst[:, k * P:(k + 1) * P],
                            XS[:, tj * D + k * P: tj * D + (k + 1) * P], CON)
                for k in range(NF)][-1]),
                waits=[tok_x, tok_c, (dve, dve.n)], incs=[(pe, 1)])
        p.op("vector", lambda e: (e.tensor_copy(XFa, ps["A"][:, 0:D]),
                                  e.tensor_copy(XFb, ps["Bp"][:, 0:D]))[-1],
             waits=[pe.tok])
        tok_xf = (p.sems["dve"], p.sems["dve"].n)
        for wi, wname in enumerate(["Wqk", "Wv", "Wu"]):
            mm_toks = []
            for fc in range(NF):
                w = [mm_toks[fc - 2]] if fc >= 2 else [prev_pe]
                p.dma("sync", wb[fc % 2], io[wname][fc * P:(fc + 1) * P, :],
                      ldwp[fc % 2], waits=w)
                tok_w = ldwp[fc % 2].tok

                def qmm(e, fc=fc):
                    ins = None
                    for half in range(2):
                        src = XFa if half == 0 else XFb
                        dst = ps["A"] if half == 0 else ps["Bp"]
                        for nb in range(4):
                            ins = e.matmul(dst[:, nb * 512:(nb + 1) * 512],
                                           src[:, fc * P:(fc + 1) * P],
                                           wb[fc % 2][:, nb * 512:(nb + 1) * 512],
                                           start=(fc == 0), stop=(fc == NF - 1))
                    return ins
                p.op("tensor", qmm, waits=[tok_w, tok_xf] +
                     ([(dve, dve.n)] if fc == 0 else []), incs=[(pe, 1)])
                mm_toks.append(pe.tok)
            prev_pe = mm_toks[-1]
            for half in range(2):
                tj = 2 * pr + half
                srcp = (ps["A"] if half == 0 else ps["Bp"])[:, 0:D]
                if wi == 0:
                    def f1(e, srcp=srcp):
                        e.tensor_add(scr, srcp, biases[:, 0:D])
                        return e.tensor_scalar_min(featp[0] if False else scr2,
                                                   scr, 0.0)
                    fb = featp[half]
                    p.op("vector", (lambda e, srcp=srcp: e.tensor_add(
                        scr, srcp, biases[:, 0:D])),
                        waits=[prev_pe, (act, act.n)])
                    p.op("vector", lambda e, fb=fb: e.tensor_scalar_min(fb, scr, 0.0))
                    p.op("scalar", lambda e, fb=fb: e.activation(fb, fb, AF.Exp),
                         waits=[(dve, dve.n)])
                    tok_e = (p.sems["act"], p.sems["act"].n)
                    p.op("vector", lambda e: e.tensor_scalar_max(scr, scr, 0.0),
                         waits=[])
                    p.op("vector", (lambda e, fb=fb: e.scalar_tensor_tensor(
                        fb, scr, -1.0, fb, ALU.add, ALU.add)), waits=[tok_e])
                    tok_feat = (p.sems["dve"], p.sems["dve"].n)
                    p.dma("gpsimd", io["feat"][tj * P:(tj + 1) * P, :], fb, sto,
                          waits=[tok_feat])
                elif wi == 1:
                    vb_ = vp[half]
                    p.op("vector", (lambda e, srcp=srcp: e.tensor_add(
                        scr, srcp, biases[:, D:2 * D])),
                        waits=[prev_pe, (act, act.n)])
                    p.op("scalar", (lambda e, vb_=vb_: e.activation(
                        vb_, scr, AF.Silu)), waits=[(dve, dve.n)])
                else:
                    p.op("vector", (lambda e, srcp=srcp: e.tensor_add(
                        scr, srcp, biases[:, 2 * D:3 * D])),
                        waits=[prev_pe, (act, act.n)])
                    p.op("scalar", lambda e: e.activation(scr, scr, AF.Silu),
                         waits=[(dve, dve.n)])
                    p.op("gpsimd", (lambda e, tj=tj: e.dma_start(
                        out=io["u"][tj * P:(tj + 1) * P, :], in_=scr)),
                        waits=[(p.sems["act"], p.sems["act"].n)],
                        incs=[(sto, 16)], is_dma=True)
                    p.op("vector", lambda e: e.engine_nop(), waits=[sto.tok])
        for half in range(2):
            def kvmm(e, half=half):
                ins = None
                for h in range(NF):
                    ins = e.matmul(ps["A"][:, h * P:(h + 1) * P],
                                   featp[half][:, h * P:(h + 1) * P],
                                   vp[half][:, h * P:(h + 1) * P],
                                   start=True, stop=True)
                return ins
            p.op("tensor", kvmm, waits=[(dve, dve.n), (act, act.n)], incs=[(pe, 1)])
            p.op("vector", lambda e: e.tensor_add(kvacc, kvacc, ps["A"][:, 0:D]),
                 waits=[pe.tok])
        prev_pe = pe.tok
    p.dma("gpsimd", io["kvpart"], kvacc, sto, waits=[(dve, dve.n)])
    p.op("gpsimd", lambda e: e.engine_nop(), waits=[sto.tok])


def build_l1a_part1():
    nc = bass.Bass("TRN2", target_bir_lowering=False)
    io = {}
    io["xin"] = nc.dram_tensor("xin", [TOK, D], F32, kind="ExternalInput").ap()
    for nm in ["Wqk", "Wv", "Wu"]:
        io[nm] = nc.dram_tensor(nm, [D, D], F32, kind="ExternalInput").ap()
    for nm in ["bqk", "bv", "bu"]:
        io[nm] = nc.dram_tensor(nm, [P, D], F32, kind="ExternalInput").ap()
    io["ident"] = nc.dram_tensor("ident", [P, P], F32, kind="ExternalInput").ap()
    io["feat"] = nc.dram_tensor("feat", [TOK, D], F32, kind="ExternalOutput").ap()
    io["u"] = nc.dram_tensor("u", [TOK, D], F32, kind="ExternalOutput").ap()
    io["kvpart"] = nc.dram_tensor("kvpart", [P, D], F32, kind="ExternalOutput").ap()
    est = contextlib.ExitStack()
    with est:
        SB = est.enter_context(nc.sbuf_tensor("SB", [P, 44000], F32))
        ps = {"A": est.enter_context(nc.psum_tensor("psA", [P, 2048], F32)),
              "Bp": est.enter_context(nc.psum_tensor("psB", [P, 2048], F32))}
        ar = Arena(SB, 44000, "f32")
        p = Prog(nc, est)
        _emit_l1a_part1(p, ar, io, ps)
        p.finalize()
    return nc


def _emit_l1a_part2(p, ar, io, ps):
    ldx, ldc = p.sem("ldx"), p.sem("ldc")
    pe, dve, act = p.sem("pe"), p.sem("dve"), p.sem("act")
    ldwp = [p.sem("ldw0"), p.sem("ldw1")]
    ldf = p.sem("ldf")
    sto = p.sem("sto")

    XS = ar.alloc(TPC * D, "XS")
    CON = ar.alloc(P, "ident")
    brep = ar.alloc(3 * D, "breps")
    kv = ar.alloc(D, "kv")
    stats = ar.alloc(16, "stats")
    for j in range(TPC):
        p.dma("sync", XS[:, j * D:(j + 1) * D], io["xin"][j * P:(j + 1) * P, :], ldx)
    p.dma("sync", CON, io["ident"], ldc)
    p.dma("sync", brep[:, 0:D], io["bo"], ldc)
    p.dma("sync", brep[:, D:2 * D], io["lng"], ldc)
    p.dma("sync", brep[:, 2 * D:3 * D], io["lnb"], ldc)
    kvp = ar.alloc(D, "kvp")
    p.dma("sync", kvp, io["kvparts"][0], ldc)
    p.op("vector", lambda e: e.tensor_copy(kv, kvp), waits=[ldc.tok])
    for g in range(1, 4):
        p.dma("sync", kvp, io["kvparts"][g], ldc, waits=[(dve, p.sems["dve"].n)])
        p.op("vector", lambda e: e.tensor_add(kv, kv, kvp), waits=[ldc.tok])
    tok_kv = (p.sems["dve"], p.sems["dve"].n)
    tok_c = ldc.tok
    tok_x = ldx.tok

    fb = ar.alloc(D, "fb")
    ub = ar.alloc(D, "ub")
    yt = ar.alloc(D, "yt")
    yn = ar.alloc(D, "yn")
    sq = ar.alloc(D, "sq")
    ysum = ar.alloc(D, "ysum")
    wb = [ar.alloc(D, "wb0"), ar.alloc(D, "wb1")]

    prev_pe = pe.tok
    for j in range(TPC):
        p.dma("sync", fb, io["feat"][j * P:(j + 1) * P, :], ldf,
              waits=[prev_pe])
        p.dma("sync", ub, io["u"][j * P:(j + 1) * P, :], ldf)
        tok_f = ldf.tok
        featF = fb
        p.op("tensor", (lambda e: [
            e.transpose(ps["Bp"][:, k * P:(k + 1) * P], fb[:, k * P:(k + 1) * P],
                        CON) for k in range(NF)][-1]),
            waits=[tok_f, tok_c, (dve, dve.n)], incs=[(pe, 1)])
        p.op("vector", lambda e: e.tensor_copy(yt, ps["Bp"][:, 0:D]),
             waits=[pe.tok])
        tok_ff = (p.sems["dve"], p.sems["dve"].n)
        def omm(e):
            ins = None
            for h in range(NF):
                ins = e.matmul(ps["A"][:, h * P:(h + 1) * P],
                               yt[:, h * P:(h + 1) * P],
                               kv[:, h * P:(h + 1) * P], start=True, stop=True)
            return ins
        p.op("tensor", omm, waits=[tok_ff, tok_kv], incs=[(pe, 1)])
        p.op("vector", lambda e: e.tensor_copy(yt, ps["A"][:, 0:D]),
             waits=[pe.tok])
        tok_y = (p.sems["dve"], p.sems["dve"].n)
        emit_ln(p, yt, yn, sq, stats, 1e-5, D, wait_y=[tok_y],
                affine=(brep[:, D:2 * D], brep[:, 2 * D:3 * D]), extra_mul=ub)
        p.op("tensor", (lambda e: [
            e.transpose(ps["Bp"][:, k * P:(k + 1) * P], yn[:, k * P:(k + 1) * P],
                        CON) for k in range(NF)][-1]),
            waits=[(dve, p.sems["dve"].n), tok_c], incs=[(pe, 1)])
        p.op("vector", lambda e: e.tensor_copy(yt, ps["Bp"][:, 0:D]),
             waits=[pe.tok])
        tok_yf = (p.sems["dve"], p.sems["dve"].n)
        mm_toks = []
        for fc in range(NF):
            w = [mm_toks[fc - 2]] if fc >= 2 else [prev_pe]
            p.dma("sync", wb[fc % 2], io["Wo"][fc * P:(fc + 1) * P, :],
                  ldwp[fc % 2], waits=w)
            tok_w = ldwp[fc % 2].tok

            def wmm(e, fc=fc):
                ins = None
                for nb in range(4):
                    ins = e.matmul(ps["A"][:, nb * 512:(nb + 1) * 512],
                                   yt[:, fc * P:(fc + 1) * P],
                                   wb[fc % 2][:, nb * 512:(nb + 1) * 512],
                                   start=(fc == 0), stop=(fc == NF - 1))
                return ins
            p.op("tensor", wmm, waits=[tok_w, tok_yf] +
                 ([(dve, p.sems["dve"].n)] if fc == 0 else []), incs=[(pe, 1)])
            mm_toks.append(pe.tok)
        prev_pe = mm_toks[-1]
        p.op("vector", (lambda e, j=j: (
            e.tensor_add(ysum, ps["A"][:, 0:D], brep[:, 0:D]),
            e.tensor_add(ysum, ysum, XS[:, j * D:(j + 1) * D]))[-1]),
            waits=[prev_pe])
        emit_ln(p, ysum, XS[:, j * D:(j + 1) * D], sq, stats, 1e-5, D,
                wait_y=[(p.sems["dve"], p.sems["dve"].n)])
    for j in range(TPC):
        p.dma("gpsimd", io["xout"][j * P:(j + 1) * P, :], XS[:, j * D:(j + 1) * D],
              sto, waits=[(dve, p.sems["dve"].n)])
    p.op("gpsimd", lambda e: e.engine_nop(), waits=[sto.tok])


def build_l1a_part2():
    nc = bass.Bass("TRN2", target_bir_lowering=False)
    io = {}
    io["xin"] = nc.dram_tensor("xin", [TOK, D], F32, kind="ExternalInput").ap()
    io["feat"] = nc.dram_tensor("feat", [TOK, D], F32, kind="ExternalInput").ap()
    io["u"] = nc.dram_tensor("u", [TOK, D], F32, kind="ExternalInput").ap()
    io["kvparts"] = nc.dram_tensor("kvparts", [4, P, D], F32, kind="ExternalInput").ap()
    io["Wo"] = nc.dram_tensor("Wo", [D, D], F32, kind="ExternalInput").ap()
    io["bo"] = nc.dram_tensor("bo", [P, D], F32, kind="ExternalInput").ap()
    io["lng"] = nc.dram_tensor("lng", [P, D], F32, kind="ExternalInput").ap()
    io["lnb"] = nc.dram_tensor("lnb", [P, D], F32, kind="ExternalInput").ap()
    io["ident"] = nc.dram_tensor("ident", [P, P], F32, kind="ExternalInput").ap()
    io["xout"] = nc.dram_tensor("xout", [TOK, D], F32, kind="ExternalOutput").ap()
    est = contextlib.ExitStack()
    with est:
        SB = est.enter_context(nc.sbuf_tensor("SB", [P, 46000], F32))
        ps = {"A": est.enter_context(nc.psum_tensor("psA", [P, 2048], F32)),
              "Bp": est.enter_context(nc.psum_tensor("psB", [P, 2048], F32))}
        ar = Arena(SB, 46000, "f32")
        p = Prog(nc, est)
        _emit_l1a_part2(p, ar, io, ps)
        p.finalize()
    return nc



NU = 18
GW = 1408


def _emit_l1_ffn(p, ar, io, ps):
    ldx, ldc = p.sem("ldx"), p.sem("ldc")
    pe, dve, act = p.sem("pe"), p.sem("dve"), p.sem("act")
    ldwp = [p.sem("ldw0"), p.sem("ldw1")]
    sto = p.sem("sto")

    XF = ar.alloc(NF * TOK, "XF")
    yacc = ar.alloc(TPC * D, "yacc")
    hF = ar.alloc(TPC * GW, "hF")
    gu8 = ar.alloc(TPC * 512, "gu8")
    CON = ar.alloc(P, "ident")
    gws = ar.alloc(NF * 16, "gateT")
    iot = ar.alloc(16, "iota16")
    cmb = ar.alloc(TPC * 32, "cmb")
    stats = ar.alloc(16, "stats")
    xb = ar.alloc(D, "xb")
    G = ar.alloc(64, "gscr")
    S16 = ar.alloc(16, "S16")
    T16 = ar.alloc(16, "T16")
    M16 = ar.alloc(16, "M16")
    M1b = ar.alloc(16, "M1b")
    M2b = ar.alloc(16, "M2b")
    wbk = [ar.alloc(512, "wb0"), ar.alloc(512, "wb1")]
    sq = gu8[:, 0:D]

    p.dma("sync", CON, io["ident"], ldc)
    p.dma("sync", gws.rearrange("p (f e) -> p f e", f=NF),
          io["gateT"].rearrange("(f p) e -> p f e", p=P), ldc)
    p.dma("sync", iot, io["iota16"], ldc)
    tok_c = ldc.tok
    p.op("vector", lambda e: e.memset(yacc, 0.0))
    p.op("vector", lambda e: e.memset(cmb, 1.0))

    for j in range(TPC):
        p.dma("sync", xb, io["xin"][j * P:(j + 1) * P, :], ldx,
              waits=[(dve, p.sems["dve"].n)])
        tok_xb = ldx.tok
        for r in range(2):
            p.op("tensor", (lambda e, r=r: [
                e.transpose(ps["A"][:, (k % 8) * P:(k % 8 + 1) * P],
                            xb[:, k * P:(k + 1) * P], CON)
                for k in range(8 * r, 8 * r + 8)][-1]),
                waits=[tok_xb, tok_c, (dve, p.sems["dve"].n)], incs=[(pe, 1)])
            p.op("vector", (lambda e, j=j, r=r: [
                e.tensor_copy(XF[:, (8 * r + k) * TOK + j * P:
                                 (8 * r + k) * TOK + (j + 1) * P],
                              ps["A"][:, k * P:(k + 1) * P])
                for k in range(8)][-1]), waits=[pe.tok])

        def gmm(e, j=j):
            ins = None
            for fc in range(NF):
                ins = e.matmul(ps["Bp"][:, 0:16],
                               XF[:, fc * TOK + j * P: fc * TOK + (j + 1) * P],
                               gws[:, fc * 16:(fc + 1) * 16],
                               start=(fc == 0), stop=(fc == NF - 1))
            return ins
        p.op("tensor", gmm, waits=[(dve, p.sems["dve"].n)], incs=[(pe, 1)])
        tok_g = pe.tok
        p.chain("vector", [
            lambda e: e.tensor_reduce(G[:, 0:1], ps["Bp"][:, 0:16], axis=AX.X,
                                      op=ALU.max),
            lambda e: e.tensor_scalar_mul(G[:, 1:2], G[:, 0:1], -1.0),
        ], waits=[tok_g])
        p.op("scalar", lambda e: e.activation(S16, ps["Bp"][:, 0:16], AF.Exp,
                                              bias=G[:, 1:2]),
             waits=[(dve, p.sems["dve"].n)])
        tok_e = (act, p.sems["act"].n)
        p.chain("vector", [
            lambda e: e.tensor_reduce(G[:, 2:3], S16, axis=AX.X, op=ALU.add),
            lambda e: e.reciprocal(G[:, 3:4], G[:, 2:3]),
            lambda e: e.tensor_scalar_mul(T16, S16, G[:, 3:4]),
            lambda e: e.tensor_reduce(G[:, 4:5], T16, axis=AX.X, op=ALU.max),
            lambda e: e.tensor_single_scalar(M16, T16, G[:, 4:5], ALU.is_ge),
            lambda e: e.scalar_tensor_tensor(M16, M16, -99.0, iot, ALU.mult,
                                             ALU.add),
            lambda e: e.tensor_scalar_add(M16, M16, 99.0),
            lambda e: e.tensor_reduce(G[:, 5:6], M16, axis=AX.X, op=ALU.min),
            lambda e: e.tensor_single_scalar(M1b, iot, G[:, 5:6], ALU.is_equal),
            lambda e: e.scalar_tensor_tensor(S16, M1b, -1e30, T16, ALU.mult,
                                             ALU.add),
            lambda e: e.tensor_reduce(G[:, 6:7], S16, axis=AX.X, op=ALU.max),
            lambda e: e.tensor_single_scalar(M16, S16, G[:, 6:7], ALU.is_ge),
            lambda e: e.scalar_tensor_tensor(M16, M16, -99.0, iot, ALU.mult,
                                             ALU.add),
            lambda e: e.tensor_scalar_add(M16, M16, 99.0),
            lambda e: e.tensor_reduce(G[:, 7:8], M16, axis=AX.X, op=ALU.min),
            lambda e: e.tensor_single_scalar(M2b, iot, G[:, 7:8], ALU.is_equal),
            lambda e: e.tensor_single_scalar(M1b, M1b, G[:, 4:5], ALU.mult),
            lambda e: e.tensor_single_scalar(M2b, M2b, G[:, 6:7], ALU.mult),
            (lambda e, j=j: e.tensor_add(cmb[:, j * 32:j * 32 + 16], M1b, M2b)),
        ], waits=[tok_e])
    tok_xf_all = (dve, p.sems["dve"].n)

    def pbank(t, w=512):
        base = (ps["A"] if t < 4 else ps["Bp"])
        return base[:, (t % 4) * 512:(t % 4) * 512 + w]

    prev_unit_tok = (dve, p.sems["dve"].n)
    last_mm = (pe, p.sems["pe"].n)
    for ui in range(NU):
        hf_toks = []
        for cp in range(3):
            w = 512 if cp < 2 else 384
            gc0 = cp * 512
            mm_toks = []
            for fc in range(NF):
                wt = [mm_toks[fc - 2]] if fc >= 2 else [last_mm]
                p.dma("sync", wbk[fc % 2][:, 0:w],
                      io["wgu"][ui, fc * P:(fc + 1) * P, gc0:gc0 + w],
                      ldwp[fc % 2], waits=wt)
                tok_w = ldwp[fc % 2].tok

                def gmm2(e, fc=fc, w=w):
                    ins = None
                    for t in range(TPC):
                        ins = e.matmul(pbank(t, w),
                                       XF[:, fc * TOK + t * P: fc * TOK + (t + 1) * P],
                                       wbk[fc % 2][:, 0:w],
                                       start=(fc == 0), stop=(fc == NF - 1))
                    return ins
                gw_ = [tok_w, tok_xf_all]
                if fc == 0:
                    gw_.append(prev_unit_tok)
                    gw_.append(hf_toks[-1] if hf_toks else prev_unit_tok)
                    gw_.append((dve, p.sems["dve"].n))
                p.op("tensor", gmm2, waits=gw_, incs=[(pe, 1)])
                mm_toks.append(pe.tok)
            p.op("vector", (lambda e, w=w: [
                e.tensor_copy(gu8[:, t * 512:t * 512 + w], pbank(t, w))
                for t in range(TPC)][-1]), waits=[mm_toks[-1]])
            tok_gev = (dve, p.sems["dve"].n)
            last_mm = mm_toks[-1]
            mm_toks = []
            for fc in range(NF):
                wt = [mm_toks[fc - 2]] if fc >= 2 else [last_mm]
                p.dma("sync", wbk[fc % 2][:, 0:w],
                      io["wgu"][ui, fc * P:(fc + 1) * P, GW + gc0:GW + gc0 + w],
                      ldwp[fc % 2], waits=wt)
                tok_w = ldwp[fc % 2].tok

                def umm(e, fc=fc, w=w):
                    ins = None
                    for t in range(TPC):
                        ins = e.matmul(pbank(t, w),
                                       XF[:, fc * TOK + t * P: fc * TOK + (t + 1) * P],
                                       wbk[fc % 2][:, 0:w],
                                       start=(fc == 0), stop=(fc == NF - 1))
                    return ins
                uw = [tok_w]
                if fc == 0:
                    uw.append(tok_gev)
                p.op("tensor", umm, waits=uw, incs=[(pe, 1)])
                mm_toks.append(pe.tok)
            tok_umm = mm_toks[-1]
            last_mm = tok_umm
            p.op("scalar", (lambda e, w=w: [
                e.activation(gu8[:, t * 512:t * 512 + w],
                             gu8[:, t * 512:t * 512 + w], AF.Silu)
                for t in range(TPC)][-1]), waits=[tok_gev])
            p.op("vector", (lambda e, w=w: [
                e.tensor_mul(gu8[:, t * 512:t * 512 + w],
                             gu8[:, t * 512:t * 512 + w], pbank(t, w))
                for t in range(TPC)][-1]),
                waits=[tok_umm, (act, p.sems["act"].n)])
            tok_h = (dve, p.sems["dve"].n)
            nch = w // P
            for t in range(TPC):
                p.op("tensor", (lambda e, t=t, nch=nch: [
                    e.transpose(ps["A"][:, k * P:(k + 1) * P],
                                gu8[:, t * 512 + k * P: t * 512 + (k + 1) * P],
                                CON) for k in range(nch)][-1]),
                    waits=[tok_h] + ([hf_toks[-1]] if (t == 0 and hf_toks) else []) +
                          ([(dve, p.sems["dve"].n)] if t > 0 else []),
                    incs=[(pe, 1)])
                p.op("vector", (lambda e, t=t, cp=cp, nch=nch: [
                    e.tensor_copy(hF[:, t * GW + cp * 512 + k * P:
                                     t * GW + cp * 512 + (k + 1) * P],
                                  ps["A"][:, k * P:(k + 1) * P])
                    for k in range(nch)][-1]), waits=[pe.tok])
                hf_toks.append((dve, p.sems["dve"].n))
        down_toks = []
        for yc in range(4):
            mm_toks = []
            for ic in range(11):
                wt = [mm_toks[ic - 2]] if ic >= 2 else [last_mm]
                p.dma("sync", wbk[ic % 2][:, 0:512],
                      io["wd"][ui, ic * P:(ic + 1) * P, yc * 512:(yc + 1) * 512],
                      ldwp[ic % 2], waits=wt)
                tok_w = ldwp[ic % 2].tok

                def dmm(e, ic=ic):
                    ins = None
                    for t in range(TPC):
                        ins = e.matmul(pbank(t),
                                       hF[:, t * GW + ic * P: t * GW + (ic + 1) * P],
                                       wbk[ic % 2][:, 0:512],
                                       start=(ic == 0), stop=(ic == 10))
                    return ins
                dw = [tok_w, hf_toks[-1]]
                if ic == 0:
                    dw.append((dve, p.sems["dve"].n))
                p.op("tensor", dmm, waits=dw, incs=[(pe, 1)])
                mm_toks.append(pe.tok)
                last_mm = pe.tok
            p.op("vector", (lambda e, yc=yc, ui=ui: [
                e.scalar_tensor_tensor(
                    yacc[:, t * D + yc * 512: t * D + (yc + 1) * 512],
                    pbank(t), cmb[:, t * 32 + ui: t * 32 + ui + 1],
                    yacc[:, t * D + yc * 512: t * D + (yc + 1) * 512],
                    ALU.mult, ALU.add) for t in range(TPC)][-1]),
                waits=[mm_toks[-1]])
            down_toks.append((dve, p.sems["dve"].n))
        prev_unit_tok = down_toks[-1]

    prev_sto = None
    for j in range(TPC):
        p.dma("sync", xb, io["xin"][j * P:(j + 1) * P, :], ldx,
              waits=[prev_sto if prev_sto else (dve, p.sems["dve"].n)])
        tok_xb = ldx.tok
        p.op("vector", (lambda e, j=j: e.tensor_add(
            yacc[:, j * D:(j + 1) * D], yacc[:, j * D:(j + 1) * D], xb)),
            waits=[tok_xb, prev_unit_tok])
        emit_ln(p, yacc[:, j * D:(j + 1) * D], xb, sq, stats, 1e-5, D,
                wait_y=[(dve, p.sems["dve"].n)])
        p.dma("gpsimd", io["xout"][j * P:(j + 1) * P, :], xb, sto,
              waits=[(dve, p.sems["dve"].n)])
        prev_sto = sto.tok


def build_l1_ffn():
    nc = bass.Bass("TRN2", target_bir_lowering=False)
    io = {}
    io["xin"] = nc.dram_tensor("xin", [TOK, D], F32, kind="ExternalInput").ap()
    io["wgu"] = nc.dram_tensor("wgu", [NU, D, 2 * GW], F32, kind="ExternalInput").ap()
    io["wd"] = nc.dram_tensor("wd", [NU, GW, D], F32, kind="ExternalInput").ap()
    io["gateT"] = nc.dram_tensor("gateT", [D, 16], F32, kind="ExternalInput").ap()
    io["iota16"] = nc.dram_tensor("iota16", [P, 16], F32, kind="ExternalInput").ap()
    io["ident"] = nc.dram_tensor("ident", [P, P], F32, kind="ExternalInput").ap()
    io["xout"] = nc.dram_tensor("xout", [TOK, D], F32, kind="ExternalOutput").ap()
    est = contextlib.ExitStack()
    with est:
        SB = est.enter_context(nc.sbuf_tensor("SB", [P, 53100], F32))
        ps = {"A": est.enter_context(nc.psum_tensor("psA", [P, 2048], F32)),
              "Bp": est.enter_context(nc.psum_tensor("psB", [P, 2048], F32))}
        ar = Arena(SB, 53100, "f32")
        p = Prog(nc, est)
        _emit_l1_ffn(p, ar, io, ps)
        p.finalize()
    return nc


def host_moe_weights(w_gate_up, w_down, shared_gate_up, shared_down):
    wgu = np.zeros((NU, D, 2 * GW), dtype=np.float32)
    wd = np.zeros((NU, GW, D), dtype=np.float32)
    wgu[:16] = w_gate_up
    wd[:16] = w_down
    for h in range(2):
        wgu[16 + h, :, 0:GW] = shared_gate_up[:, h * GW:(h + 1) * GW]
        wgu[16 + h, :, GW:2 * GW] = shared_gate_up[:, 2 * GW + h * GW: 2 * GW + (h + 1) * GW]
        wd[16 + h] = shared_down[h * GW:(h + 1) * GW, :]
    return wgu, wd



def kernel(**inputs):
    import ml_dtypes
    x = np.asarray(inputs["L0_attn_x"], dtype=np.float32)
    ident = np.eye(P, dtype=np.float32)
    identb = np.eye(P, dtype=ml_dtypes.bfloat16)
    ones = np.ones((P, P), dtype=np.float32)

    def rep(b):
        return np.ascontiguousarray(
            np.broadcast_to(np.asarray(b, dtype=np.float32), (P, D)))

    cores = list(range(NCORES))

    nc = build_l0_attn()
    in_maps = l0_attn_inputs(x, np.asarray(inputs["L0_attn_wq"], dtype=np.float32),
                             np.asarray(inputs["L0_attn_wk"], dtype=np.float32),
                             np.asarray(inputs["L0_attn_wv"], dtype=np.float32),
                             np.asarray(inputs["L0_attn_wo"], dtype=np.float32))
    res = run_bass_kernel_spmd(nc, in_maps, core_ids=cores)
    xs = [np.asarray(r["xout"]) for r in res.results]

    nc = build_l0_ffn()
    wfc = np.ascontiguousarray(inputs["L0_ffn_w_fc"], dtype=np.float32)
    wpr = np.ascontiguousarray(inputs["L0_ffn_w_proj"], dtype=np.float32)
    in_maps = [dict(xin=xs[c], wfc=wfc, wpr=wpr, ident=ident, identb=identb,
                    ones=ones) for c in cores]
    res = run_bass_kernel_spmd(nc, in_maps, core_ids=cores)
    xs = [np.asarray(r["xout"]) for r in res.results]

    nc = build_l1a_part1()
    Wqk = np.ascontiguousarray(inputs["L1_attn_Wqk"], dtype=np.float32)
    Wv = np.ascontiguousarray(inputs["L1_attn_Wv"], dtype=np.float32)
    Wu = np.ascontiguousarray(inputs["L1_attn_Wu"], dtype=np.float32)
    bqk_r = rep(inputs["L1_attn_bqk"])
    bv_r = rep(inputs["L1_attn_bv"])
    bu_r = rep(inputs["L1_attn_bu"])
    in_maps = [dict(xin=xs[c], Wqk=Wqk, Wv=Wv, Wu=Wu, bqk=bqk_r, bv=bv_r,
                    bu=bu_r, ident=ident) for c in cores]
    res = run_bass_kernel_spmd(nc, in_maps, core_ids=cores)
    feats = [np.asarray(r["feat"]) for r in res.results]
    us = [np.asarray(r["u"]) for r in res.results]
    kvps = [np.asarray(r["kvpart"]) for r in res.results]

    nc = build_l1a_part2()
    Wo1 = np.ascontiguousarray(inputs["L1_attn_Wo"], dtype=np.float32)
    bo_r = rep(inputs["L1_attn_bo"])
    lng_r = rep(inputs["L1_attn_ln_g"])
    lnb_r = rep(inputs["L1_attn_ln_b"])
    in_maps = []
    for c in cores:
        b = c // 4
        kvparts = np.stack([kvps[4 * b + g] for g in range(4)])
        in_maps.append(dict(xin=xs[c], feat=feats[c], u=us[c], kvparts=kvparts,
                            Wo=Wo1, bo=bo_r, lng=lng_r, lnb=lnb_r, ident=ident))
    res = run_bass_kernel_spmd(nc, in_maps, core_ids=cores)
    xs = [np.asarray(r["xout"]) for r in res.results]

    nc = build_l1_ffn()
    wgu, wd = host_moe_weights(
        np.asarray(inputs["L1_ffn_w_gate_up"], dtype=np.float32),
        np.asarray(inputs["L1_ffn_w_down"], dtype=np.float32),
        np.asarray(inputs["L1_ffn_shared_gate_up"], dtype=np.float32),
        np.asarray(inputs["L1_ffn_shared_down"], dtype=np.float32))
    gateT = np.ascontiguousarray(np.asarray(inputs["L1_ffn_gate_w"],
                                            dtype=np.float32).T)
    iota16 = np.ascontiguousarray(
        np.broadcast_to(np.arange(16, dtype=np.float32), (P, 16)))
    in_maps = [dict(xin=xs[c], wgu=wgu, wd=wd, gateT=gateT, iota16=iota16,
                    ident=ident) for c in cores]
    res = run_bass_kernel_spmd(nc, in_maps, core_ids=cores)
    xs = [np.asarray(r["xout"]) for r in res.results]

    return unshard_rows(xs).reshape(B, SEQ, D)
